# version 01
# baseline (speedup 1.0000x reference)
import numpy as np
import concourse.bass as bass
import concourse.mybir as mybir
from concourse.bass_utils import run_bass_kernel_spmd
from contextlib import ExitStack

F32 = mybir.dt.float32
U32 = mybir.dt.uint32
AF = mybir.ActivationFunctionType
ALU = mybir.AluOpType
AX = mybir.AxisListType.X

B_PER_CORE = 2
S = 2048
D = 1024
NT = S // 128
TT = 512
NTT = S // TT
K_TOP = S // 2
BIS_ITERS = 48
EPS = 1e-5

SUBLAYERS = [(0, "ffn"), (0, "attn"), (0, "ffn"), (1, "ffn"), (1, "attn"), (1, "ffn")]
PARTS = len(SUBLAYERS)
DBG = False
LN_ONLY = False
LN_NT = NT
NB_RUN = B_PER_CORE


def _host_consts():
    c = {}
    c["eye"] = np.eye(128, dtype=np.float32)
    pxor = np.zeros((128, 128), dtype=np.float32)
    for k in range(128):
        pxor[k, k ^ 1] = 1.0
    c["pxor"] = pxor
    c["cones"] = np.ones((128, 128), dtype=np.float32)
    bd2 = np.zeros((128, 2), dtype=np.float32)
    bd2[:64, 0] = 1.0
    bd2[64:, 1] = 1.0
    c["bd2"] = bd2
    arep = np.zeros((2, 128), dtype=np.float32)
    arep[0, :64] = 1.0
    arep[1, 64:] = 1.0
    c["arep"] = arep
    half = 32
    pos = np.arange(S, dtype=np.float32)[:, None]
    freq = (1.0 / (10000.0 ** (np.arange(half, dtype=np.float32) / np.float32(half)))).astype(np.float32)
    ang = (pos * freq[None, :]).astype(np.float32)
    sin = np.sin(ang).astype(np.float32)
    cos = np.cos(ang).astype(np.float32)
    cosT = np.empty((64, S), dtype=np.float32)
    sinT = np.empty((64, S), dtype=np.float32)
    for p in range(64):
        i = p // 2
        cosT[p] = cos[:, i]
        sinT[p] = (sin[:, i] if (p % 2 == 1) else -sin[:, i])
    c["cosT"] = np.tile(cosT, (2, 1)).astype(np.float32)
    c["sinT"] = np.tile(sinT, (2, 1)).astype(np.float32)
    c["epsc"] = np.full((128, 1), EPS, dtype=np.float32)
    return c


def build_nc():
    nc = bass.Bass("TRN2", target_bir_lowering=False)
    xin = nc.dram_tensor("xin", [B_PER_CORE, S, D], F32, kind="ExternalInput").ap()
    yout = nc.dram_tensor("yout", [B_PER_CORE, S, D], F32, kind="ExternalOutput").ap()
    wd = {}
    for name, shape in [
        ("l0_wq", [D, D]), ("l0_wkv", [D, 2 * D]), ("l0_wout", [D, D]),
        ("l0_bout", [1, D]), ("l0_scale", [16, 1]),
        ("wr_rep", [128, D]), ("l0_w1", [D, 4 * D]), ("l0_b1c", [128, 32]),
        ("l0_w2", [4 * D, D]), ("l0_b2c", [128, 8]),
        ("l1_qkv", [D, 3 * D]), ("l1_qkvbc", [128, 24]),
        ("l1_wout", [D, D]), ("l1_boutr", [1, D]),
        ("l1_cbt", [D, D]), ("l1_scbc", [128, 8]), ("l1_biasc", [128, 8]),
        ("eye", [128, 128]), ("pxor", [128, 128]), ("cones", [128, 128]),
        ("bd2", [128, 2]), ("arep", [2, 128]),
        ("cosT", [128, S]), ("sinT", [128, S]), ("epsc", [128, 1]),
    ]:
        wd[name] = nc.dram_tensor(name, shape, F32, kind="ExternalInput").ap()
    xnt_d = nc.dram_tensor("xnt_d", [8, 128, S], F32, kind="Internal").ap()
    esd = nc.dram_tensor("esd", [16, 1], F32, kind="Internal").ap()
    dbg = nc.dram_tensor("dbg", [128, 16384], F32, kind="ExternalOutput").ap() if DBG else None
    dbg_done = [False]

    ops = {"tensor": [], "vector": [], "scalar": [], "gpsimd": [], "sync": []}
    CNT = {"T": 0, "V": 0, "A": 0, "G": 0, "DS": 0, "DG": 0}
    psum_uid = [0]

    def op(eng, f, inc=None, waits=None):
        wl = [(s, v) for (s, v) in (waits or []) if v > 0]
        amt = None
        if inc is not None:
            amt = 16 if inc in ("DS", "DG") else 1
            CNT[inc] += amt
        nm = inc
        drain = nm in ("V", "A", "G") and eng in ("vector", "scalar", "gpsimd")

        def run(e, sems):
            for s, v in wl:
                e.wait_ge(sems[s], v)
            r = f(e)
            if nm is not None:
                if drain:
                    e.drain().then_inc(sems[nm], amt)
                else:
                    r.then_inc(sems[nm], amt)
        ops[eng].append(run)

    def now(*names):
        return [(n, CNT[n]) for n in names]

    es = ExitStack()

    def sbuf(name, shape, dt=F32):
        return es.enter_context(nc.sbuf_tensor(name, shape, dt))

    def psum(shape):
        psum_uid[0] += 1
        return nc.psum_tensor("ps%d" % psum_uid[0], shape, F32)

    xs = sbuf("xs", [128, NT * D])
    eye = sbuf("eye_s", [128, 128])
    pxor = sbuf("pxor_s", [128, 128])
    cones = sbuf("cones_s", [128, 128])
    bd2 = sbuf("bd2_s", [128, 2])
    arep = sbuf("arep_s", [2, 128])
    epsc = sbuf("epsc_s", [128, 1])
    wr_rep = sbuf("wr_rep_s", [128, D])
    b1c = sbuf("b1c_s", [128, 32])
    b2c = sbuf("b2c_s", [128, 8])
    qkvbc = sbuf("qkvbc_s", [128, 24])
    scbc = sbuf("scbc_s", [128, 8])
    scb127 = sbuf("scb127_s", [128, 8])
    l1biasc = sbuf("l1biasc_s", [128, 8])
    scale16 = sbuf("scale16_s", [16, 1])
    eS = sbuf("eS_s", [16, 1])
    eSp = sbuf("eSp_s", [2, 8])
    r_all = sbuf("r_all", [128, NT])
    nm_all = sbuf("nm_all", [128, NT])
    m_all = sbuf("m_all", [128, NT])
    sumc = sbuf("sumc", [128, 2])
    s_all = sbuf("s_all", [128, NT])
    w_mod = sbuf("w_mod", [128, NT])
    msku = sbuf("msku", [128, NT], U32)
    mskf = sbuf("mskf", [128, NT])
    bis = sbuf("bis", [1, 8])
    bisu = sbuf("bisu", [1, 2], U32)
    thrc = sbuf("thrc", [128, 1])
    cntf = sbuf("cntf", [128, 1])
    cm1 = sbuf("cm1", [128, 1])
    xn_buf = sbuf("xn_buf", [128, D])
    stg = sbuf("stg", [128, D])
    AR = sbuf("arena", [128, 28672])

    def ar(off, n):
        return AR[:, off:off + n]

    def dump(dst_off, src_ap_fn, n):
        if not DBG or dbg_done[0]:
            return
        op("gpsimd", lambda e: e.dma_start(
            out=dbg[:, dst_off:dst_off + n], in_=src_ap_fn(),
            allow_slow_non_contiguous=True),
           waits=now("V", "A", "T"), inc="DG")

    for name, tile in [("eye", eye), ("pxor", pxor), ("cones", cones),
                       ("bd2", bd2), ("arep", arep),
                       ("epsc", epsc), ("wr_rep", wr_rep),
                       ("l0_b1c", b1c), ("l0_b2c", b2c), ("l1_qkvbc", qkvbc),
                       ("l1_scbc", scbc), ("l1_biasc", l1biasc),
                       ("l0_scale", scale16)]:
        op("sync", lambda e, t=tile, n=name: e.dma_start(out=t[:], in_=wd[n]), inc="DS")
    op("vector", lambda e: e.tensor_scalar(
        scb127[:], scbc[:], float(1.0 / 127.0), None, op0=ALU.mult),
       waits=now("DS"), inc="V")
    op("scalar", lambda e: e.activation(eS[:], scale16[:], AF.Exp),
       waits=now("DS"), inc="A")
    op("sync", lambda e: e.dma_start(out=esd, in_=eS[:]), waits=now("A"), inc="DS")
    op("sync", lambda e: e.dma_start(
        out=eSp[:], in_=esd.rearrange("(p r) o -> r (p o)", r=2),
        allow_slow_non_contiguous=True), inc="DS")

    def emit_ln():
        for t in range(LN_NT):
            xt = xs[:, t * D:(t + 1) * D]
            op("scalar", lambda e, xt=xt: e.activation(
                xn_buf[:], xt, AF.Square),
               waits=now("V"), inc="A")
            aA = CNT["A"]
            op("vector", lambda e, xt=xt: e.reduce_sum(sumc[:, 0:1], xt, axis=AX), inc="V")
            op("vector", lambda e: e.reduce_sum(sumc[:, 1:2], xn_buf[:], axis=AX),
               waits=[("A", aA)], inc="V")
            op("vector", lambda e, t=t: e.tensor_scalar(
                m_all[:, t:t + 1], sumc[:, 0:1], float(1.0 / D), None, op0=ALU.mult),
               waits=now("V"), inc="V")
            op("vector", lambda e, t=t: e.tensor_tensor(
                cm1[:], m_all[:, t:t + 1], m_all[:, t:t + 1], op=ALU.mult), inc="V")
            op("vector", lambda e: e.scalar_tensor_tensor(
                cm1[:], sumc[:, 1:2], float(1.0 / D), cm1[:],
                op0=ALU.mult, op1=ALU.subtract), inc="V")
            if DBG:
                op("vector", lambda e, t=t: e.tensor_copy(w_mod[:, t:t + 1], cm1[:]),
                   waits=now("V"), inc="V")
                op("vector", lambda e, t=t: e.tensor_copy(mskf[:, t:t + 1], sumc[:, 1:2]), inc="V")
            vV = CNT["V"]
            op("scalar", lambda e: e.activation(cm1[:], cm1[:], AF.Sqrt,
                                                scale=1.0, bias=epsc[:]),
               waits=[("V", vV)], inc="A")
            aA2 = CNT["A"]
            op("vector", lambda e, t=t: e.reciprocal(r_all[:, t:t + 1], cm1[:]),
               waits=[("A", aA2)], inc="V")
            op("vector", lambda e, t=t: e.scalar_tensor_tensor(
                nm_all[:, t:t + 1], m_all[:, t:t + 1], -1.0, r_all[:, t:t + 1],
                op0=ALU.mult, op1=ALU.mult), waits=now("V"), inc="V")

    def emit_mod_router():
        for t in range(NT):
            xt = xs[:, t * D:(t + 1) * D]
            op("scalar", lambda e, xt=xt, t=t: e.activation(
                xn_buf[:], xt, AF.Identity, scale=r_all[:, t:t + 1],
                bias=nm_all[:, t:t + 1]), waits=now("V"), inc="A")
            aA = CNT["A"]
            op("vector", lambda e: e.tensor_tensor(
                xn_buf[:], xn_buf[:], wr_rep[:], op=ALU.mult),
               waits=[("A", aA)], inc="V")
            op("vector", lambda e, t=t: e.reduce_sum(
                s_all[:, t:t + 1], xn_buf[:], axis=AX), inc="V")
        op("vector", lambda e: e.tensor_scalar(mskf[:], s_all[:], -1.0, None,
                                               op0=ALU.mult), waits=now("V"), inc="V")
        op("vector", lambda e: e.tensor_tensor(mskf[:], s_all[:], mskf[:],
                                               op=ALU.max), inc="V")
        op("vector", lambda e: e.reduce_max(cm1[:], mskf[:], axis=AX), inc="V")
        vv = CNT["V"]
        with psum([1, 128]) as pbr, psum([128, 1]) as pcr, psum([1, 1]) as pcnt:
            op("tensor", lambda e: e.transpose(pbr[:], cm1[:], eye[:]),
               waits=[("V", vv), ("A", CNT["A"])], inc="T")
            tT = CNT["T"]
            op("vector", lambda e: e.reduce_max(bis[:, 1:2], pbr[:], axis=AX),
               waits=[("T", tT)], inc="V")
            op("vector", lambda e: e.tensor_scalar(bis[:, 1:2], bis[:, 1:2], 1.0,
                                                   None, op0=ALU.add),
               waits=now("V"), inc="V")
            op("vector", lambda e: e.tensor_scalar(bis[:, 0:1], bis[:, 1:2], -1.0,
                                                   None, op0=ALU.mult), inc="V")
            op("vector", lambda e: e.memset(bis[:, 2:3], 0.0), inc="V")
            vv2 = CNT["V"]
            op("tensor", lambda e: e.matmul(pcr[:], cones[0:1, :], bis[:, 2:3],
                                            start=True, stop=True),
               waits=[("V", vv2)], inc="T")
            op("vector", lambda e: e.tensor_copy(thrc[:], pcr[:]),
               waits=now("T"), inc="V")
            for it in range(BIS_ITERS):
                op("vector", lambda e: e.tensor_scalar(
                    msku[:], s_all[:], thrc[:], None, op0=ALU.is_ge), inc="V")
                op("vector", lambda e: e.tensor_copy(mskf[:], msku[:]), inc="V")
                op("vector", lambda e: e.reduce_sum(cntf[:], mskf[:], axis=AX), inc="V")
                vc = CNT["V"]
                op("tensor", lambda e: e.matmul(pcnt[:], cones[:, 0:1], cntf[:],
                                                start=True, stop=True),
                   waits=[("V", vc)], inc="T")
                tc = CNT["T"]
                op("vector", lambda e: e.tensor_scalar(
                    bisu[:, 0:1], pcnt[:], float(K_TOP), None, op0=ALU.is_ge),
                   waits=[("T", tc)], inc="V")
                op("vector", lambda e: e.select(
                    bis[:, 0:1], bisu[:, 0:1], bis[:, 2:3], bis[:, 0:1]), inc="V")
                op("vector", lambda e: e.select(
                    bis[:, 1:2], bisu[:, 0:1], bis[:, 1:2], bis[:, 2:3]), inc="V")
                op("vector", lambda e: e.tensor_tensor(
                    bis[:, 3:4], bis[:, 0:1], bis[:, 1:2], op=ALU.add), inc="V")
                op("vector", lambda e: e.tensor_scalar(
                    bis[:, 2:3], bis[:, 3:4], 0.5, None, op0=ALU.mult), inc="V")
                vm = CNT["V"]
                src = (lambda: bis[:, 2:3]) if it < BIS_ITERS - 1 else (lambda: bis[:, 0:1])
                op("tensor", lambda e, src=src: e.matmul(
                    pcr[:], cones[0:1, :], src(), start=True, stop=True),
                   waits=[("V", vm)], inc="T")
                tr = CNT["T"]
                op("vector", lambda e: e.tensor_copy(thrc[:], pcr[:]),
                   waits=[("T", tr)], inc="V")
            op("vector", lambda e: e.tensor_scalar(
                msku[:], s_all[:], thrc[:], None, op0=ALU.is_ge), inc="V")
            op("vector", lambda e: e.tensor_copy(mskf[:], msku[:]), inc="V")
            vv3 = CNT["V"]
            op("scalar", lambda e: e.activation(w_mod[:], s_all[:], AF.Sigmoid),
               waits=[("V", vv3)], inc="A")
            aA = CNT["A"]
            op("vector", lambda e: e.tensor_tensor(
                w_mod[:], w_mod[:], mskf[:], op=ALU.mult),
               waits=[("A", aA)], inc="V")

    def emit_xnT_tiles(tiles, dst_off, dst_cols, pes):
        for tix, t in enumerate(tiles):
            xt = xs[:, t * D:(t + 1) * D]
            op("scalar", lambda e, xt=xt, t=t: e.activation(
                xn_buf[:], xt, AF.Identity, scale=r_all[:, t:t + 1],
                bias=nm_all[:, t:t + 1]), waits=now("V"), inc="A")
            axn = CNT["A"]
            for c in range(8):
                pe = pes[(tix * 8 + c) % 2]
                wl = [("A", axn), ("V", CNT["V"])]
                op("tensor", lambda e, pe=pe, c=c: e.transpose(
                    pe[:], xn_buf[:, c * 128:(c + 1) * 128], eye[:]),
                   waits=wl, inc="T")
                tT = CNT["T"]
                op("vector", lambda e, pe=pe, c=c, tix=tix: e.tensor_copy(
                    ar(dst_off + c * dst_cols + tix * 128, 128), pe[:]),
                   waits=[("T", tT)], inc="V")

    def emit_ffn(layer, is_mod):
        emit_ln()
        if is_mod:
            emit_mod_router()
        X_XNT, X_H, X_FACC, X_W1, X_W2 = 0, 4096, 8192, 12288, 20480
        dump(0, lambda: r_all[:], NT)
        dump(16, lambda: nm_all[:], NT)
        dump(65, lambda: m_all[:], NT)
        dump(81, lambda: w_mod[:], NT)
        dump(97, lambda: mskf[:], NT)
        if LN_ONLY:
            dbg_done[0] = True
            op("vector", lambda e: e.tensor_copy(xs[:, 0:NT], r_all[:]),
               waits=now("V", "A"), inc="V")
            op("vector", lambda e: e.tensor_copy(xs[:, NT:2 * NT], m_all[:]), inc="V")
            op("vector", lambda e: e.tensor_copy(xs[:, 2 * NT:3 * NT], w_mod[:]), inc="V")
            op("vector", lambda e: e.tensor_copy(xs[:, 3 * NT:4 * NT], mskf[:]), inc="V")
            return
        if is_mod:
            dump(32, lambda: s_all[:], NT)
            dump(48, lambda: thrc[:], 1)
            dump(49, lambda: w_mod[:], NT)
        arena_free = now("T", "V", "A")
        if not is_mod:
            for k in range(8):
                op("sync", lambda e, k=k: e.dma_start(
                    out=ar(X_W1 + k * 1024, 1024),
                    in_=wd["l1_cbt"][k * 128:(k + 1) * 128, :]),
                   waits=(arena_free if k == 0 else None), inc="DS")
        cbt_ds = CNT["DS"]
        w_T_done = {0: 0, 1: 0}
        first_wdma = [True]

        with psum([128, 128]) as pt0, psum([128, 128]) as pt1, \
             psum([128, 512]) as ph0, psum([128, 512]) as ph1, \
             psum([128, 512]) as ph2, psum([128, 512]) as ph3, \
             psum([128, 512]) as po0, psum([128, 512]) as po1:
            phs = [ph0, ph1, ph2, ph3]
            pos_ = [po0, po1]
            for tt in range(NTT):
                tiles = [tt * 4 + i for i in range(4)]
                emit_xnT_tiles(tiles, X_XNT, TT, [pt0, pt1])
                xnt_v = CNT["V"]
                if is_mod:
                    for fg in range(8):
                        buf = fg % 2
                        wwait = [("T", w_T_done[buf])]
                        if first_wdma[0]:
                            wwait = arena_free + wwait
                            first_wdma[0] = False
                        for k in range(8):
                            op("sync", lambda e, k=k, fg=fg, buf=buf: e.dma_start(
                                out=ar(X_W1 + buf * 4096 + k * 512, 512),
                                in_=wd["l0_w1"][k * 128:(k + 1) * 128,
                                                fg * 512:(fg + 1) * 512]),
                               waits=(wwait if k == 0 else None), inc="DS")
                        for hh in range(4):
                            op("sync", lambda e, hh=hh, fg=fg, buf=buf: e.dma_start(
                                out=ar(X_W2 + buf * 4096 + hh * 1024, 1024),
                                in_=wd["l0_w2"][(fg * 4 + hh) * 128:(fg * 4 + hh + 1) * 128, :]),
                               inc="DS")
                        wds = CNT["DS"]
                        for fi in range(4):
                            for k in range(8):
                                wl = None
                                if k == 0:
                                    wl = [("A", CNT["A"])]
                                    if fi == 0:
                                        wl += [("DS", wds), ("V", xnt_v)]
                                op("tensor", lambda e, fi=fi, k=k, buf=buf: e.matmul(
                                    phs[fi][:],
                                    ar(X_W1 + buf * 4096 + k * 512 + fi * 128, 128),
                                    ar(X_XNT + k * 512, 512),
                                    start=(k == 0), stop=(k == 7)),
                                   waits=wl, inc="T")
                        th = CNT["T"]
                        for fi in range(4):
                            fabs = fg * 4 + fi
                            op("scalar", lambda e, fi=fi, fabs=fabs: e.activation(
                                ar(X_H + fi * 512, 512), phs[fi][:],
                                AF.Gelu_apprx_tanh, scale=1.0,
                                bias=b1c[:, fabs:fabs + 1]),
                               waits=[("T", th)], inc="A")
                        ah = CNT["A"]
                        for oc in range(8):
                            pp = pos_[oc % 2]
                            for hh in range(4):
                                wl = None
                                if hh == 0:
                                    wl = [("A", ah), ("V", CNT["V"])]
                                op("tensor", lambda e, oc=oc, hh=hh, buf=buf, pp=pp: e.matmul(
                                    pp[:],
                                    ar(X_W2 + buf * 4096 + hh * 1024 + oc * 128, 128),
                                    ar(X_H + hh * 512, 512),
                                    start=(hh == 0), stop=(hh == 3)),
                                   waits=wl, inc="T")
                            to = CNT["T"]
                            if fg == 0:
                                op("vector", lambda e, oc=oc, pp=pp: e.tensor_copy(
                                    ar(X_FACC + oc * 512, 512), pp[:]),
                                   waits=[("T", to)], inc="V")
                            else:
                                op("vector", lambda e, oc=oc, pp=pp: e.tensor_add(
                                    ar(X_FACC + oc * 512, 512),
                                    ar(X_FACC + oc * 512, 512), pp[:]),
                                   waits=[("T", to)], inc="V")
                        w_T_done[buf] = CNT["T"]
                else:
                    for oc in range(8):
                        pp = pos_[oc % 2]
                        for k in range(8):
                            wl = None
                            if k == 0:
                                wl = [("A", CNT["A"])]
                                if oc == 0:
                                    wl += [("DS", cbt_ds), ("V", xnt_v)]
                            op("tensor", lambda e, oc=oc, k=k, pp=pp: e.matmul(
                                pp[:], ar(X_W1 + k * 1024 + oc * 128, 128),
                                ar(X_XNT + k * 512, 512),
                                start=(k == 0), stop=(k == 7)),
                               waits=wl, inc="T")
                        to = CNT["T"]
                        op("scalar", lambda e, oc=oc, pp=pp: e.activation(
                            ar(X_FACC + oc * 512, 512), pp[:], AF.Identity,
                            scale=scb127[:, oc:oc + 1], bias=l1biasc[:, oc:oc + 1]),
                           waits=[("T", to)], inc="A")
                if tt == 0:
                    dump(128, lambda: ar(X_XNT, 4096), 4096)
                    dump(4224, lambda: ar(X_H, 4096), 4096)
                    dump(8320, lambda: ar(X_FACC, 4096), 4096)
                    dbg_done[0] = True
                facc_done = now("V", "A")
                for tix, t in enumerate(tiles):
                    for c in range(8):
                        pe = [ph0, ph1][c % 2]
                        wl = facc_done + [("V", CNT["V"]), ("A", CNT["A"])]
                        op("tensor", lambda e, c=c, tix=tix, pe=pe: e.transpose(
                            pe[:, 0:128], ar(X_FACC + c * 512 + tix * 128, 128), eye[:]),
                           waits=wl, inc="T")
                        tT = CNT["T"]
                        op("vector", lambda e, c=c, pe=pe: e.tensor_copy(
                            stg[:, c * 128:(c + 1) * 128], pe[:, 0:128]),
                           waits=[("T", tT)], inc="V")
                    vstg = CNT["V"]
                    if is_mod:
                        op("scalar", lambda e, t=t: e.activation(
                            xn_buf[:], xs[:, t * D:(t + 1) * D], AF.Identity,
                            scale=r_all[:, t:t + 1], bias=nm_all[:, t:t + 1]),
                           waits=[("V", vstg)], inc="A")
                        op("scalar", lambda e, t=t: e.activation(
                            stg[:], stg[:], AF.Identity, scale=w_mod[:, t:t + 1]),
                           inc="A")
                        aup = CNT["A"]
                        op("vector", lambda e: e.scalar_tensor_tensor(
                            xn_buf[:], xn_buf[:], 1.0, stg[:],
                            op0=ALU.mult, op1=ALU.add),
                           waits=[("A", aup)], inc="V")
                        op("vector", lambda e, t=t: e.scalar_tensor_tensor(
                            xs[:, t * D:(t + 1) * D], xn_buf[:], 0.5,
                            xs[:, t * D:(t + 1) * D], op0=ALU.mult, op1=ALU.add),
                           inc="V")
                    else:
                        op("vector", lambda e, t=t: e.scalar_tensor_tensor(
                            xs[:, t * D:(t + 1) * D], stg[:], 0.5,
                            xs[:, t * D:(t + 1) * D], op0=ALU.mult, op1=ALU.add),
                           waits=[("V", vstg)], inc="V")

    def emit_attn(layer):
        emit_ln()
        A_QT, A_KT, A_VT, A_PV, A_O = 0, 2048, 4096, 6144, 8192
        A_WQKV, A_WOUT, A_RT, A_PT2, A_RNQ, A_OT = 10240, 13312, 14336, 14848, 15872, 15904
        A_XS = 16384
        A_COS, A_SIN, A_BROW = 18432, 20480, 22528
        arena_free = now("T", "V", "A")
        op("sync", lambda e: e.dma_start(
            out=AR[0:1, A_BROW:A_BROW + D],
            in_=wd["l0_bout" if layer == 0 else "l1_boutr"]),
           waits=arena_free, inc="DS")
        if layer == 1:
            op("sync", lambda e: e.dma_start(out=ar(A_COS, S), in_=wd["cosT"]), inc="DS")
            op("sync", lambda e: e.dma_start(out=ar(A_SIN, S), in_=wd["sinT"]), inc="DS")
        with psum([128, 128]) as pt0, psum([128, 128]) as pt1:
            for t in range(NT):
                op("scalar", lambda e, t=t: e.activation(
                    xn_buf[:], xs[:, t * D:(t + 1) * D], AF.Identity,
                    scale=r_all[:, t:t + 1], bias=nm_all[:, t:t + 1]),
                   waits=now("V"), inc="A")
                axn = CNT["A"]
                for c in range(8):
                    pe = [pt0, pt1][c % 2]
                    wl = [("A", axn), ("V", CNT["V"])]
                    if c == 0:
                        wl.append(("DS", CNT["DS"]))
                    op("tensor", lambda e, pe=pe, c=c: e.transpose(
                        pe[:], xn_buf[:, c * 128:(c + 1) * 128], eye[:]),
                       waits=wl, inc="T")
                    tT = CNT["T"]
                    op("vector", lambda e, pe=pe, c=c: e.tensor_copy(
                        stg[:, c * 128:(c + 1) * 128], pe[:]),
                       waits=[("T", tT)], inc="V")
                vstg = CNT["V"]
                op("sync", lambda e, t=t: e.dma_start(
                    out=xnt_d[:, :, t * 128:(t + 1) * 128].rearrange("c p k -> p c k"),
                    in_=stg[:].rearrange("p (c k) -> p c k", k=128)),
                   waits=[("V", vstg)], inc="DS")
        spill_ds = CNT["DS"]

        for pair in range(8):
            wfree = [("T", CNT["T"])] + (arena_free if pair == 0 else [])
            if layer == 0:
                srcs = [wd["l0_wq"][:, pair * 128:(pair + 1) * 128],
                        wd["l0_wkv"][:, pair * 128:(pair + 1) * 128],
                        wd["l0_wkv"][:, D + pair * 128:D + (pair + 1) * 128]]
            else:
                srcs = [wd["l1_qkv"][:, pair * 128:(pair + 1) * 128],
                        wd["l1_qkv"][:, D + pair * 128:D + (pair + 1) * 128],
                        wd["l1_qkv"][:, 2 * D + pair * 128:2 * D + (pair + 1) * 128]]
            for pi, src in enumerate(srcs):
                op("sync", lambda e, pi=pi, src=src: e.dma_start(
                    out=ar(A_WQKV + pi * 1024, 1024).rearrange("p (k f) -> p k f", f=128),
                    in_=src.rearrange("(k p) f -> p k f", p=128)),
                   waits=(wfree if pi == 0 else None), inc="DS")
            op("sync", lambda e, pair=pair: e.dma_start(
                out=ar(A_WOUT, 1024),
                in_=wd["l0_wout" if layer == 0 else "l1_wout"][pair * 128:(pair + 1) * 128, :]),
               inc="DS")
            wds = CNT["DS"]
            with psum([128, 1024]) as pq, psum([128, 1024]) as pk, psum([128, 1024]) as pv:
                for th in range(2):
                    for c in range(8):
                        sbf = c % 2
                        wl = [("DS", spill_ds), ("T", CNT["T"])]
                        op("sync", lambda e, c=c, th=th, sbf=sbf: e.dma_start(
                            out=ar(A_XS + sbf * 1024, 1024),
                            in_=xnt_d[c, :, th * 1024:(th + 1) * 1024]),
                           waits=wl, inc="DS")
                        xds = CNT["DS"]
                        for pi, pp in enumerate([pq, pk, pv]):
                            for mv in range(2):
                                wl2 = None
                                if pi == 0 and mv == 0:
                                    wl2 = [("DS", xds)]
                                    if c == 0:
                                        wl2.append(("A", CNT["A"]))
                                op("tensor", lambda e, pi=pi, mv=mv, c=c, sbf=sbf, pp=pp: e.matmul(
                                    pp[:, mv * 512:(mv + 1) * 512],
                                    ar(A_WQKV + pi * 1024 + c * 128, 128),
                                    ar(A_XS + sbf * 1024 + mv * 512, 512),
                                    start=(c == 0), stop=(c == 7),
                                    skip_group_check=True),
                                   waits=wl2, inc="T")
                    tproj = CNT["T"]
                    for pi, (dst, pp) in enumerate([(A_QT, pq), (A_KT, pk), (A_PV, pv)]):
                        if layer == 1:
                            bc = qkvbc[:, pi * 8 + pair:pi * 8 + pair + 1]
                            op("scalar", lambda e, dst=dst, th=th, pp=pp, bc=bc: e.activation(
                                ar(dst + th * 1024, 1024), pp[:], AF.Identity,
                                scale=1.0, bias=bc),
                               waits=[("T", tproj)], inc="A")
                        else:
                            op("scalar", lambda e, dst=dst, th=th, pp=pp: e.copy(
                                ar(dst + th * 1024, 1024), pp[:]),
                               waits=[("T", tproj)], inc="A")
            aproj = CNT["A"]
            with psum([128, 512]) as ptv:
                for g in range(4):
                    wl = [("A", aproj), ("V", CNT["V"])]
                    for i in range(4):
                        kt = g * 4 + i
                        op("tensor", lambda e, kt=kt, i=i, ptv=ptv: e.transpose(
                            ptv[:, i * 128:(i + 1) * 128],
                            ar(A_PV + kt * 128, 128), eye[:]),
                           waits=(wl if i == 0 else None), inc="T")
                    tg = CNT["T"]
                    op("vector", lambda e, g=g, ptv=ptv: e.tensor_copy(
                        ar(A_VT + g * 512, 512), ptv[:]),
                       waits=[("T", tg)], inc="V")

            if layer == 0:
                rn_q = AR[0:2, A_O:A_O + 2048]
                rn_k = AR[0:2, A_XS:A_XS + 2048]
                with psum([2, 512]) as pn0, psum([2, 512]) as pn1, \
                     psum([128, 512]) as pnr, psum([128, 2]) as pnt:
                    pns = [pn0, pn1]
                    for src, rn in [(A_QT, rn_q), (A_KT, rn_k)]:
                        for i in range(4):
                            op("scalar", lambda e, src=src, i=i: e.activation(
                                stg[:, 0:512], ar(src + i * 512, 512), AF.Square),
                               waits=now("V", "T"), inc="A")
                            asq = CNT["A"]
                            pn = pns[i % 2]
                            op("tensor", lambda e, pn=pn: e.matmul(
                                pn[:], bd2[:], stg[:, 0:512], start=True, stop=True),
                               waits=[("A", asq)], inc="T")
                            tn = CNT["T"]
                            op("scalar", lambda e, rn=rn, i=i, pn=pn: e.activation(
                                rn[:, i * 512:(i + 1) * 512], pn[:], AF.Sqrt),
                               waits=[("T", tn)], inc="A")
                    az = CNT["A"]
                    op("vector", lambda e: e.tensor_scalar(
                        rn_q[:], rn_q[:], 1e-12, None, op0=ALU.max),
                       waits=[("A", az)], inc="V")
                    op("vector", lambda e: e.tensor_scalar(
                        rn_k[:], rn_k[:], 1e-12, None, op0=ALU.max), inc="V")
                    op("vector", lambda e: e.reciprocal(rn_q[:], rn_q[:]), inc="V")
                    op("vector", lambda e: e.reciprocal(rn_k[:], rn_k[:]), inc="V")
                    op("vector", lambda e, pair=pair: e.tensor_scalar(
                        rn_q[:], rn_q[:], eSp[:, pair:pair + 1], None, op0=ALU.mult), inc="V")
                    vrn = CNT["V"]
                    for i in range(4):
                        op("tensor", lambda e, i=i, pnr=pnr: e.matmul(
                            pnr[:], arep[:], rn_k[:, i * 512:(i + 1) * 512],
                            start=True, stop=True),
                           waits=[("V", vrn if i == 0 else CNT["V"])], inc="T")
                        tr = CNT["T"]
                        op("vector", lambda e, i=i, pnr=pnr: e.tensor_tensor(
                            ar(A_KT + i * 512, 512), ar(A_KT + i * 512, 512),
                            pnr[:], op=ALU.mult),
                           waits=[("T", tr)], inc="V")
                    for qt in range(16):
                        op("tensor", lambda e, qt=qt, pnt=pnt: e.transpose(
                            pnt[:], rn_q[:, qt * 128:(qt + 1) * 128],
                            eye[0:2, 0:2]),
                           waits=[("V", CNT["V"])], inc="T")
                        tq = CNT["T"]
                        op("vector", lambda e, qt=qt, pnt=pnt: e.tensor_copy(
                            ar(A_RNQ + qt * 2, 2), pnt[:]),
                           waits=[("T", tq)], inc="V")
            else:
                with psum([128, 512]) as pr:
                    for src in (A_QT, A_KT):
                        for i in range(4):
                            op("tensor", lambda e, src=src, i=i, pr=pr: e.matmul(
                                pr[:], pxor[:], ar(src + i * 512, 512),
                                start=True, stop=True),
                               waits=[("A", aproj), ("V", CNT["V"])], inc="T")
                            tp = CNT["T"]
                            op("vector", lambda e, i=i: e.tensor_tensor(
                                ar(A_RT, 512), pr[:], ar(A_SIN + i * 512, 512),
                                op=ALU.mult),
                               waits=[("T", tp)], inc="V")
                            op("vector", lambda e, src=src, i=i: e.tensor_tensor(
                                ar(src + i * 512, 512), ar(src + i * 512, 512),
                                ar(A_COS + i * 512, 512), op=ALU.mult), inc="V")
                            op("vector", lambda e, src=src, i=i: e.tensor_add(
                                ar(src + i * 512, 512), ar(src + i * 512, 512),
                                ar(A_RT, 512)), inc="V")
            prep_v = CNT["V"]
            prep_a = CNT["A"]

            with psum([128, 2048]) as psim, psum([128, 512]) as ppt, \
                 psum([128, 64]) as po, psum([128, 1024]) as py:
                for h in range(2):
                    hb = h * 64
                    for qt in range(16):
                        wl = ([("V", prep_v), ("A", prep_a)]
                              if (h == 0 and qt == 0) else [("A", CNT["A"])])
                        for ks in range(4):
                            op("tensor", lambda e, hb=hb, qt=qt, ks=ks, psim=psim: e.matmul(
                                psim[:, ks * 512:(ks + 1) * 512],
                                AR[hb:hb + 64, A_QT + qt * 128:A_QT + qt * 128 + 128],
                                AR[hb:hb + 64, A_KT + ks * 512:A_KT + (ks + 1) * 512],
                                start=True, stop=True, skip_group_check=True),
                               waits=(wl if ks == 0 else None), inc="T")
                        tsim = CNT["T"]
                        op("vector", lambda e, psim=psim: e.reduce_max(
                            cm1[:], psim[:], axis=AX),
                           waits=[("T", tsim)], inc="V")
                        if layer == 0:
                            cq = ar(A_RNQ + qt * 2 + h, 1)
                            op("vector", lambda e, cq=cq: e.scalar_tensor_tensor(
                                sumc[:, 0:1], cm1[:], -1.0, cq,
                                op0=ALU.mult, op1=ALU.mult),
                               waits=[("A", CNT["A"])] + now("V"), inc="V")
                            vexp = CNT["V"]
                            op("scalar", lambda e, cq=cq, psim=psim: e.activation(
                                ar(A_PV, 2048), psim[:], AF.Exp, scale=cq,
                                bias=sumc[:, 0:1]),
                               waits=[("V", vexp)], inc="A")
                        else:
                            op("vector", lambda e: e.tensor_scalar(
                                sumc[:, 0:1], cm1[:], -0.125, None, op0=ALU.mult),
                               waits=[("A", CNT["A"])] + now("V"), inc="V")
                            vexp = CNT["V"]
                            op("scalar", lambda e, psim=psim: e.activation(
                                ar(A_PV, 2048), psim[:], AF.Exp, scale=0.125,
                                bias=sumc[:, 0:1]),
                               waits=[("V", vexp)], inc="A")
                        aexp = CNT["A"]
                        op("vector", lambda e: e.reduce_sum(sumc[:, 1:2], ar(A_PV, 2048), axis=AX),
                           waits=[("A", aexp)], inc="V")
                        op("vector", lambda e: e.reciprocal(sumc[:, 0:1], sumc[:, 1:2]),
                           waits=now("V"), inc="V")
                        for g in range(4):
                            pb = (g % 2) * 512
                            wl2 = [("A", aexp)]
                            if g >= 1:
                                wl2.append(("V", CNT["V"]))
                            for i in range(4):
                                kt = g * 4 + i
                                op("tensor", lambda e, kt=kt, i=i, ppt=ppt: e.transpose(
                                    ppt[:, i * 128:(i + 1) * 128],
                                    ar(A_PV + kt * 128, 128), eye[:]),
                                   waits=(wl2 if i == 0 else None), inc="T")
                            tg = CNT["T"]
                            op("vector", lambda e, pb=pb, ppt=ppt: e.tensor_copy(
                                ar(A_PT2 + pb, 512), ppt[:]),
                               waits=[("T", tg)], inc="V")
                            vg = CNT["V"]
                            for i in range(4):
                                kt = g * 4 + i
                                op("tensor", lambda e, kt=kt, i=i, pb=pb, hb=hb, po=po: e.matmul(
                                    po[:],
                                    ar(A_PT2 + pb + i * 128, 128),
                                    ar(A_VT + kt * 128 + hb, 64),
                                    start=(kt == 0), stop=(kt == 15),
                                    skip_group_check=True),
                                   waits=([("V", vg)] if i == 0 else None), inc="T")
                        tav = CNT["T"]
                        vr = CNT["V"]
                        op("scalar", lambda e, qt=qt, hb=hb, po=po: e.activation(
                            ar(A_O + qt * 128 + hb, 64), po[:], AF.Identity,
                            scale=sumc[:, 0:1]),
                           waits=[("T", tav), ("V", vr)], inc="A")
                oa = CNT["A"]
                for qt in range(16):
                    wl = [("A", oa)]
                    if qt > 0:
                        wl += [("T", CNT["T"]), ("V", CNT["V"])]
                    op("tensor", lambda e, qt=qt, ppt=ppt: e.transpose(
                        ppt[:, 0:128], ar(A_O + qt * 128, 128), eye[:]),
                       waits=wl, inc="T")
                    tot = CNT["T"]
                    op("vector", lambda e, ppt=ppt: e.tensor_copy(
                        ar(A_OT, 128), ppt[:, 0:128]),
                       waits=[("T", tot)], inc="V")
                    vot = CNT["V"]
                    for mv in range(2):
                        op("tensor", lambda e, mv=mv, py=py: e.matmul(
                            py[:, mv * 512:(mv + 1) * 512],
                            ar(A_OT, 128),
                            ar(A_WOUT + mv * 512, 512),
                            start=True, stop=(pair != 0),
                            skip_group_check=True),
                           waits=([("V", vot), ("DS", wds)] if mv == 0 else None), inc="T")
                    if pair == 0:
                        for mv in range(2):
                            op("tensor", lambda e, mv=mv, py=py: e.matmul(
                                py[:, mv * 512:(mv + 1) * 512],
                                cones[0:1, :],
                                AR[0:1, A_BROW + mv * 512:A_BROW + (mv + 1) * 512],
                                start=False, stop=True,
                                skip_group_check=True), inc="T")
                    ty = CNT["T"]
                    op("vector", lambda e, qt=qt, py=py: e.tensor_add(
                        xs[:, qt * D:(qt + 1) * D], xs[:, qt * D:(qt + 1) * D],
                        py[:]),
                       waits=[("T", ty)], inc="V")

    store_total = 0
    for b in range(NB_RUN):
        xwait = [("DG", CNT["DG"])] if b > 0 else []
        for t in range(NT):
            op("sync", lambda e, b=b, t=t: e.dma_start(
                out=xs[:, t * D:(t + 1) * D],
                in_=xin[b, t * 128:(t + 1) * 128, :]),
               waits=(xwait if t == 0 else None), inc="DS")
        ld_ds = CNT["DS"]
        op("vector", lambda e: e.nop(), waits=[("DS", ld_ds)], inc="V")
        op("scalar", lambda e: e.nop(), waits=[("DS", ld_ds)], inc="A")

        for (layer, kind) in SUBLAYERS[:PARTS]:
            if kind == "ffn":
                emit_ffn(layer, is_mod=(layer == 0))
            else:
                emit_attn(layer)

        vdone = now("V")
        for t in range(NT):
            op("gpsimd", lambda e, b=b, t=t: e.dma_start(
                out=yout[b, t * 128:(t + 1) * 128, :],
                in_=xs[:, t * D:(t + 1) * D]),
               waits=(vdone if t == 0 else None), inc="DG")
        store_total = CNT["DG"]

    final_dg = store_total

    with es:
        with nc.semaphore("semT") as sT, nc.semaphore("semV") as sV, \
             nc.semaphore("semA") as sA, nc.semaphore("semG") as sG, \
             nc.semaphore("semDS") as sDS, nc.semaphore("semDG") as sDG:
            _nums = sorted(x.num for x in (sT, sV, sA, sG, sDS, sDG))
            assert _nums == list(range(_nums[0], _nums[-1] + 1)), _nums
            nc.gpsimd.sem_clear(range(_nums[0], _nums[-1] + 1))
            nc.all_engine_barrier()
            block = es.enter_context(nc.Block())
            sems = {"T": sT, "V": sV, "A": sA, "G": sG, "DS": sDS, "DG": sDG}

            @block.tensor
            def _(e):
                for f in ops["tensor"]:
                    f(e, sems)

            @block.vector
            def _(e):
                for f in ops["vector"]:
                    f(e, sems)

            @block.scalar
            def _(e):
                for f in ops["scalar"]:
                    f(e, sems)

            @block.sync
            def _(e):
                for f in ops["sync"]:
                    f(e, sems)

            @block.gpsimd
            def _(e):
                for f in ops["gpsimd"]:
                    f(e, sems)
                e.wait_ge(sems["DG"], final_dg)
    return nc


def kernel(**inputs):
    n = 8
    c = _host_consts()
    x = np.ascontiguousarray(np.asarray(inputs["L0_attn_x"]))
    common = {
        "l0_wq": np.ascontiguousarray(inputs["L0_attn_w_q"]),
        "l0_wkv": np.ascontiguousarray(inputs["L0_attn_w_kv"]),
        "l0_wout": np.ascontiguousarray(inputs["L0_attn_w_out"]),
        "l0_bout": np.ascontiguousarray(np.asarray(inputs["L0_attn_b_out"]).reshape(1, D)),
        "l0_scale": np.ascontiguousarray(np.asarray(inputs["L0_attn_scale"]).reshape(16, 1)),
        "wr_rep": np.ascontiguousarray(
            np.broadcast_to(np.asarray(inputs["L0_ffn_w_router"])[None, :], (128, D))),
        "l0_w1": np.ascontiguousarray(inputs["L0_ffn_W1"]),
        "l0_b1c": np.ascontiguousarray(np.asarray(inputs["L0_ffn_b1"]).reshape(32, 128).T),
        "l0_w2": np.ascontiguousarray(inputs["L0_ffn_W2"]),
        "l0_b2c": np.ascontiguousarray(np.asarray(inputs["L0_ffn_b2"]).reshape(8, 128).T),
        "l1_qkv": np.ascontiguousarray(inputs["L1_attn_qkv_w"]),
        "l1_qkvbc": np.ascontiguousarray(np.asarray(inputs["L1_attn_qkv_b"]).reshape(24, 128).T),
        "l1_wout": np.ascontiguousarray(inputs["L1_attn_out_w"]),
        "l1_boutr": np.ascontiguousarray(np.asarray(inputs["L1_attn_out_b"]).reshape(1, D)),
        "l1_cbt": np.ascontiguousarray(np.asarray(inputs["L1_ffn_CB"]).T),
        "l1_scbc": np.ascontiguousarray(np.asarray(inputs["L1_ffn_SCB"]).reshape(8, 128).T),
        "l1_biasc": np.ascontiguousarray(np.asarray(inputs["L1_ffn_bias"]).reshape(8, 128).T),
    }
    common.update({k: np.ascontiguousarray(v) for k, v in c.items()})
    nc = build_nc()
    in_maps = []
    for i in range(n):
        m = dict(common)
        m["xin"] = np.ascontiguousarray(x[2 * i:2 * i + 2])
        in_maps.append(m)
    res = run_bass_kernel_spmd(nc, in_maps, core_ids=list(range(n)))
    return np.concatenate([r["yout"] for r in res.results], axis=0)


# version 02
# speedup vs baseline: 1.0217x; 1.0217x over previous
import numpy as np
import concourse.bass as bass
import concourse.mybir as mybir
from concourse.bass_utils import run_bass_kernel_spmd
from contextlib import ExitStack

F32 = mybir.dt.float32
U32 = mybir.dt.uint32
AF = mybir.ActivationFunctionType
ALU = mybir.AluOpType
AX = mybir.AxisListType.X

B_PER_CORE = 2
S = 2048
D = 1024
NT = S // 128
TT = 512
NTT = S // TT
K_TOP = S // 2
BIS_ITERS = 48
EPS = 1e-5

SUBLAYERS = [(0, "ffn"), (0, "attn"), (0, "ffn"), (1, "ffn"), (1, "attn"), (1, "ffn")]
PARTS = len(SUBLAYERS)
DBG = False
LN_ONLY = False
LN_NT = NT
NB_RUN = B_PER_CORE


def _host_consts():
    c = {}
    c["eye"] = np.eye(128, dtype=np.float32)
    pxor = np.zeros((128, 128), dtype=np.float32)
    for k in range(128):
        pxor[k, k ^ 1] = 1.0
    c["pxor"] = pxor
    c["cones"] = np.ones((128, 128), dtype=np.float32)
    bd2 = np.zeros((128, 2), dtype=np.float32)
    bd2[:64, 0] = 1.0
    bd2[64:, 1] = 1.0
    c["bd2"] = bd2
    arep = np.zeros((2, 128), dtype=np.float32)
    arep[0, :64] = 1.0
    arep[1, 64:] = 1.0
    c["arep"] = arep
    half = 32
    pos = np.arange(S, dtype=np.float32)[:, None]
    freq = (1.0 / (10000.0 ** (np.arange(half, dtype=np.float32) / np.float32(half)))).astype(np.float32)
    ang = (pos * freq[None, :]).astype(np.float32)
    sin = np.sin(ang).astype(np.float32)
    cos = np.cos(ang).astype(np.float32)
    cosT = np.empty((64, S), dtype=np.float32)
    sinT = np.empty((64, S), dtype=np.float32)
    for p in range(64):
        i = p // 2
        cosT[p] = cos[:, i]
        sinT[p] = (sin[:, i] if (p % 2 == 1) else -sin[:, i])
    c["cosT"] = np.tile(cosT, (2, 1)).astype(np.float32)
    c["sinT"] = np.tile(sinT, (2, 1)).astype(np.float32)
    c["epsc"] = np.full((128, 1), EPS, dtype=np.float32)
    return c


def build_nc():
    nc = bass.Bass("TRN2", target_bir_lowering=False)
    xin = nc.dram_tensor("xin", [B_PER_CORE, S, D], F32, kind="ExternalInput").ap()
    yout = nc.dram_tensor("yout", [B_PER_CORE, S, D], F32, kind="ExternalOutput").ap()
    wd = {}
    for name, shape in [
        ("l0_wq", [D, D]), ("l0_wkv", [D, 2 * D]), ("l0_wout", [D, D]),
        ("l0_bout", [1, D]), ("l0_scale", [2, 8]),
        ("wr_rep", [128, D]), ("l0_w1", [D, 4 * D]), ("l0_b1c", [128, 32]),
        ("l0_w2", [4 * D, D]), ("l0_b2c", [128, 8]),
        ("l1_qkv", [D, 3 * D]), ("l1_qkvbc", [128, 24]),
        ("l1_wout", [D, D]), ("l1_boutr", [1, D]),
        ("l1_cbt", [D, D]), ("l1_scbc", [128, 8]), ("l1_biasc", [128, 8]),
        ("eye", [128, 128]), ("pxor", [128, 128]), ("cones", [128, 128]),
        ("bd2", [128, 2]), ("arep", [2, 128]),
        ("cosT", [128, S]), ("sinT", [128, S]), ("epsc", [128, 1]),
    ]:
        wd[name] = nc.dram_tensor(name, shape, F32, kind="ExternalInput").ap()
    xnt_d = nc.dram_tensor("xnt_d", [8, 128, S], F32, kind="Internal").ap()
    dbg = nc.dram_tensor("dbg", [128, 16384], F32, kind="ExternalOutput").ap() if DBG else None
    dbg_done = [False]

    ops = {"tensor": [], "vector": [], "scalar": [], "gpsimd": [], "sync": []}
    CNT = {"T": 0, "V": 0, "A": 0, "G": 0, "DS": 0, "DG": 0}
    psum_uid = [0]

    def op(eng, f, inc=None, waits=None):
        wl = [(s, v) for (s, v) in (waits or []) if v > 0]
        amt = None
        if inc is not None:
            amt = 16 if inc in ("DS", "DG") else 1
            CNT[inc] += amt
        nm = inc
        drain = nm in ("V", "A", "G") and eng in ("vector", "scalar", "gpsimd")

        def run(e, sems):
            for s, v in wl:
                e.wait_ge(sems[s], v)
            r = f(e)
            if nm is not None:
                if drain:
                    e.drain().then_inc(sems[nm], amt)
                else:
                    r.then_inc(sems[nm], amt)
        ops[eng].append(run)

    def now(*names):
        return [(n, CNT[n]) for n in names]

    es = ExitStack()

    def sbuf(name, shape, dt=F32):
        return es.enter_context(nc.sbuf_tensor(name, shape, dt))

    def psum(shape):
        psum_uid[0] += 1
        return nc.psum_tensor("ps%d" % psum_uid[0], shape, F32)

    xs = sbuf("xs", [128, NT * D])
    eye = sbuf("eye_s", [128, 128])
    pxor = sbuf("pxor_s", [128, 128])
    cones = sbuf("cones_s", [128, 128])
    bd2 = sbuf("bd2_s", [128, 2])
    arep = sbuf("arep_s", [2, 128])
    epsc = sbuf("epsc_s", [128, 1])
    wr_rep = sbuf("wr_rep_s", [128, D])
    b1c = sbuf("b1c_s", [128, 32])
    b2c = sbuf("b2c_s", [128, 8])
    qkvbc = sbuf("qkvbc_s", [128, 24])
    scbc = sbuf("scbc_s", [128, 8])
    scb127 = sbuf("scb127_s", [128, 8])
    l1biasc = sbuf("l1biasc_s", [128, 8])
    scale16 = sbuf("scale16_s", [2, 8])
    eSp = sbuf("eSp_s", [2, 8])
    r_all = sbuf("r_all", [128, NT])
    nm_all = sbuf("nm_all", [128, NT])
    m_all = sbuf("m_all", [128, NT])
    sumc = sbuf("sumc", [128, 2])
    s_all = sbuf("s_all", [128, NT])
    w_mod = sbuf("w_mod", [128, NT])
    msku = sbuf("msku", [128, NT], U32)
    mskf = sbuf("mskf", [128, NT])
    bis = sbuf("bis", [1, 8])
    bisu = sbuf("bisu", [1, 2], U32)
    thrc = sbuf("thrc", [128, 1])
    cntf = sbuf("cntf", [128, 1])
    cm1 = sbuf("cm1", [128, 1])
    xn_buf = sbuf("xn_buf", [128, D])
    stg = sbuf("stg", [128, D])
    AR = sbuf("arena", [128, 28672])

    def ar(off, n):
        return AR[:, off:off + n]

    def dump(dst_off, src_ap_fn, n):
        if not DBG or dbg_done[0]:
            return
        op("gpsimd", lambda e: e.dma_start(
            out=dbg[:, dst_off:dst_off + n], in_=src_ap_fn(),
            allow_slow_non_contiguous=True),
           waits=now("V", "A", "T"), inc="DG")

    for name, tile in [("eye", eye), ("pxor", pxor), ("cones", cones),
                       ("bd2", bd2), ("arep", arep),
                       ("epsc", epsc), ("wr_rep", wr_rep),
                       ("l0_b1c", b1c), ("l0_b2c", b2c), ("l1_qkvbc", qkvbc),
                       ("l1_scbc", scbc), ("l1_biasc", l1biasc),
                       ("l0_scale", scale16)]:
        op("sync", lambda e, t=tile, n=name: e.dma_start(out=t[:], in_=wd[n]), inc="DS")
    op("vector", lambda e: e.tensor_scalar(
        scb127[:], scbc[:], float(1.0 / 127.0), None, op0=ALU.mult),
       waits=now("DS"), inc="V")
    op("scalar", lambda e: e.activation(eSp[:], scale16[:], AF.Exp),
       waits=now("DS"), inc="A")

    def emit_ln():
        for t in range(LN_NT):
            xt = xs[:, t * D:(t + 1) * D]
            op("scalar", lambda e, xt=xt: e.activation(
                xn_buf[:], xt, AF.Square),
               waits=now("V"), inc="A")
            aA = CNT["A"]
            op("vector", lambda e, xt=xt: e.reduce_sum(sumc[:, 0:1], xt, axis=AX), inc="V")
            op("vector", lambda e: e.reduce_sum(sumc[:, 1:2], xn_buf[:], axis=AX),
               waits=[("A", aA)], inc="V")
            op("vector", lambda e, t=t: e.tensor_scalar(
                m_all[:, t:t + 1], sumc[:, 0:1], float(1.0 / D), None, op0=ALU.mult),
               waits=now("V"), inc="V")
            op("vector", lambda e, t=t: e.tensor_tensor(
                cm1[:], m_all[:, t:t + 1], m_all[:, t:t + 1], op=ALU.mult), inc="V")
            op("vector", lambda e: e.scalar_tensor_tensor(
                cm1[:], sumc[:, 1:2], float(1.0 / D), cm1[:],
                op0=ALU.mult, op1=ALU.subtract), inc="V")
            if DBG:
                op("vector", lambda e, t=t: e.tensor_copy(w_mod[:, t:t + 1], cm1[:]),
                   waits=now("V"), inc="V")
                op("vector", lambda e, t=t: e.tensor_copy(mskf[:, t:t + 1], sumc[:, 1:2]), inc="V")
            vV = CNT["V"]
            op("scalar", lambda e: e.activation(cm1[:], cm1[:], AF.Sqrt,
                                                scale=1.0, bias=epsc[:]),
               waits=[("V", vV)], inc="A")
            aA2 = CNT["A"]
            op("vector", lambda e, t=t: e.reciprocal(r_all[:, t:t + 1], cm1[:]),
               waits=[("A", aA2)], inc="V")
            op("vector", lambda e, t=t: e.scalar_tensor_tensor(
                nm_all[:, t:t + 1], m_all[:, t:t + 1], -1.0, r_all[:, t:t + 1],
                op0=ALU.mult, op1=ALU.mult), waits=now("V"), inc="V")

    def emit_mod_router():
        for t in range(NT):
            xt = xs[:, t * D:(t + 1) * D]
            op("scalar", lambda e, xt=xt, t=t: e.activation(
                xn_buf[:], xt, AF.Identity, scale=r_all[:, t:t + 1],
                bias=nm_all[:, t:t + 1]), waits=now("V"), inc="A")
            aA = CNT["A"]
            op("vector", lambda e: e.tensor_tensor(
                xn_buf[:], xn_buf[:], wr_rep[:], op=ALU.mult),
               waits=[("A", aA)], inc="V")
            op("vector", lambda e, t=t: e.reduce_sum(
                s_all[:, t:t + 1], xn_buf[:], axis=AX), inc="V")
        op("vector", lambda e: e.tensor_scalar(mskf[:], s_all[:], -1.0, None,
                                               op0=ALU.mult), waits=now("V"), inc="V")
        op("vector", lambda e: e.tensor_tensor(mskf[:], s_all[:], mskf[:],
                                               op=ALU.max), inc="V")
        op("vector", lambda e: e.reduce_max(cm1[:], mskf[:], axis=AX), inc="V")
        vv = CNT["V"]
        with psum([1, 128]) as pbr, psum([128, 1]) as pcr, psum([1, 1]) as pcnt:
            op("tensor", lambda e: e.transpose(pbr[:], cm1[:], eye[:]),
               waits=[("V", vv), ("A", CNT["A"])], inc="T")
            tT = CNT["T"]
            op("vector", lambda e: e.reduce_max(bis[:, 1:2], pbr[:], axis=AX),
               waits=[("T", tT)], inc="V")
            op("vector", lambda e: e.tensor_scalar(bis[:, 1:2], bis[:, 1:2], 1.0,
                                                   None, op0=ALU.add),
               waits=now("V"), inc="V")
            op("vector", lambda e: e.tensor_scalar(bis[:, 0:1], bis[:, 1:2], -1.0,
                                                   None, op0=ALU.mult), inc="V")
            op("vector", lambda e: e.memset(bis[:, 2:3], 0.0), inc="V")
            vv2 = CNT["V"]
            op("tensor", lambda e: e.matmul(pcr[:], cones[0:1, :], bis[:, 2:3],
                                            start=True, stop=True),
               waits=[("V", vv2)], inc="T")
            op("vector", lambda e: e.tensor_copy(thrc[:], pcr[:]),
               waits=now("T"), inc="V")
            for it in range(BIS_ITERS):
                op("vector", lambda e: e.tensor_scalar(
                    msku[:], s_all[:], thrc[:], None, op0=ALU.is_ge), inc="V")
                op("vector", lambda e: e.tensor_copy(mskf[:], msku[:]), inc="V")
                op("vector", lambda e: e.reduce_sum(cntf[:], mskf[:], axis=AX), inc="V")
                vc = CNT["V"]
                op("tensor", lambda e: e.matmul(pcnt[:], cones[:, 0:1], cntf[:],
                                                start=True, stop=True),
                   waits=[("V", vc)], inc="T")
                tc = CNT["T"]
                op("vector", lambda e: e.tensor_scalar(
                    bisu[:, 0:1], pcnt[:], float(K_TOP), None, op0=ALU.is_ge),
                   waits=[("T", tc)], inc="V")
                op("vector", lambda e: e.select(
                    bis[:, 0:1], bisu[:, 0:1], bis[:, 2:3], bis[:, 0:1]), inc="V")
                op("vector", lambda e: e.select(
                    bis[:, 1:2], bisu[:, 0:1], bis[:, 1:2], bis[:, 2:3]), inc="V")
                op("vector", lambda e: e.tensor_tensor(
                    bis[:, 3:4], bis[:, 0:1], bis[:, 1:2], op=ALU.add), inc="V")
                op("vector", lambda e: e.tensor_scalar(
                    bis[:, 2:3], bis[:, 3:4], 0.5, None, op0=ALU.mult), inc="V")
                vm = CNT["V"]
                src = (lambda: bis[:, 2:3]) if it < BIS_ITERS - 1 else (lambda: bis[:, 0:1])
                op("tensor", lambda e, src=src: e.matmul(
                    pcr[:], cones[0:1, :], src(), start=True, stop=True),
                   waits=[("V", vm)], inc="T")
                tr = CNT["T"]
                op("vector", lambda e: e.tensor_copy(thrc[:], pcr[:]),
                   waits=[("T", tr)], inc="V")
            op("vector", lambda e: e.tensor_scalar(
                msku[:], s_all[:], thrc[:], None, op0=ALU.is_ge), inc="V")
            op("vector", lambda e: e.tensor_copy(mskf[:], msku[:]), inc="V")
            vv3 = CNT["V"]
            op("scalar", lambda e: e.activation(w_mod[:], s_all[:], AF.Sigmoid),
               waits=[("V", vv3)], inc="A")
            aA = CNT["A"]
            op("vector", lambda e: e.tensor_tensor(
                w_mod[:], w_mod[:], mskf[:], op=ALU.mult),
               waits=[("A", aA)], inc="V")

    def emit_xnT_tiles(tiles, dst_off, dst_cols, pes):
        for tix, t in enumerate(tiles):
            xt = xs[:, t * D:(t + 1) * D]
            op("scalar", lambda e, xt=xt, t=t: e.activation(
                xn_buf[:], xt, AF.Identity, scale=r_all[:, t:t + 1],
                bias=nm_all[:, t:t + 1]), waits=now("V"), inc="A")
            axn = CNT["A"]
            for c in range(8):
                pe = pes[(tix * 8 + c) % 2]
                wl = [("A", axn), ("V", CNT["V"])]
                op("tensor", lambda e, pe=pe, c=c: e.transpose(
                    pe[:], xn_buf[:, c * 128:(c + 1) * 128], eye[:]),
                   waits=wl, inc="T")
                tT = CNT["T"]
                op("vector", lambda e, pe=pe, c=c, tix=tix: e.tensor_copy(
                    ar(dst_off + c * dst_cols + tix * 128, 128), pe[:]),
                   waits=[("T", tT)], inc="V")

    def emit_ffn(layer, is_mod):
        emit_ln()
        if is_mod:
            emit_mod_router()
        X_XNT, X_H, X_FACC, X_W1, X_W2 = 0, 4096, 8192, 12288, 20480
        dump(0, lambda: r_all[:], NT)
        dump(16, lambda: nm_all[:], NT)
        dump(65, lambda: m_all[:], NT)
        dump(81, lambda: w_mod[:], NT)
        dump(97, lambda: mskf[:], NT)
        if LN_ONLY:
            dbg_done[0] = True
            op("vector", lambda e: e.tensor_copy(xs[:, 0:NT], r_all[:]),
               waits=now("V", "A"), inc="V")
            op("vector", lambda e: e.tensor_copy(xs[:, NT:2 * NT], m_all[:]), inc="V")
            op("vector", lambda e: e.tensor_copy(xs[:, 2 * NT:3 * NT], w_mod[:]), inc="V")
            op("vector", lambda e: e.tensor_copy(xs[:, 3 * NT:4 * NT], mskf[:]), inc="V")
            return
        if is_mod:
            dump(32, lambda: s_all[:], NT)
            dump(48, lambda: thrc[:], 1)
            dump(49, lambda: w_mod[:], NT)
        arena_free = now("T", "V", "A")
        if not is_mod:
            for k in range(8):
                op("sync", lambda e, k=k: e.dma_start(
                    out=ar(X_W1 + k * 1024, 1024),
                    in_=wd["l1_cbt"][k * 128:(k + 1) * 128, :]),
                   waits=(arena_free if k == 0 else None), inc="DS")
        cbt_ds = CNT["DS"]
        w_T_done = {0: 0, 1: 0}
        first_wdma = [True]

        with psum([128, 128]) as pt0, psum([128, 128]) as pt1, \
             psum([128, 512]) as ph0, psum([128, 512]) as ph1, \
             psum([128, 512]) as ph2, psum([128, 512]) as ph3, \
             psum([128, 512]) as po0, psum([128, 512]) as po1:
            phs = [ph0, ph1, ph2, ph3]
            pos_ = [po0, po1]
            for tt in range(NTT):
                tiles = [tt * 4 + i for i in range(4)]
                emit_xnT_tiles(tiles, X_XNT, TT, [pt0, pt1])
                xnt_v = CNT["V"]
                if is_mod:
                    for fg in range(8):
                        buf = fg % 2
                        wwait = [("T", w_T_done[buf])]
                        if first_wdma[0]:
                            wwait = arena_free + wwait
                            first_wdma[0] = False
                        for k in range(8):
                            op("sync", lambda e, k=k, fg=fg, buf=buf: e.dma_start(
                                out=ar(X_W1 + buf * 4096 + k * 512, 512),
                                in_=wd["l0_w1"][k * 128:(k + 1) * 128,
                                                fg * 512:(fg + 1) * 512]),
                               waits=(wwait if k == 0 else None), inc="DS")
                        for hh in range(4):
                            op("sync", lambda e, hh=hh, fg=fg, buf=buf: e.dma_start(
                                out=ar(X_W2 + buf * 4096 + hh * 1024, 1024),
                                in_=wd["l0_w2"][(fg * 4 + hh) * 128:(fg * 4 + hh + 1) * 128, :]),
                               inc="DS")
                        wds = CNT["DS"]
                        for fi in range(4):
                            for k in range(8):
                                wl = None
                                if k == 0:
                                    wl = [("A", CNT["A"])]
                                    if fi == 0:
                                        wl += [("DS", wds), ("V", xnt_v)]
                                op("tensor", lambda e, fi=fi, k=k, buf=buf: e.matmul(
                                    phs[fi][:],
                                    ar(X_W1 + buf * 4096 + k * 512 + fi * 128, 128),
                                    ar(X_XNT + k * 512, 512),
                                    start=(k == 0), stop=(k == 7)),
                                   waits=wl, inc="T")
                        th = CNT["T"]
                        for fi in range(4):
                            fabs = fg * 4 + fi
                            op("scalar", lambda e, fi=fi, fabs=fabs: e.activation(
                                ar(X_H + fi * 512, 512), phs[fi][:],
                                AF.Gelu_apprx_tanh, scale=1.0,
                                bias=b1c[:, fabs:fabs + 1]),
                               waits=[("T", th)], inc="A")
                        ah = CNT["A"]
                        for oc in range(8):
                            pp = pos_[oc % 2]
                            for hh in range(4):
                                wl = None
                                if hh == 0:
                                    wl = [("A", ah), ("V", CNT["V"])]
                                op("tensor", lambda e, oc=oc, hh=hh, buf=buf, pp=pp: e.matmul(
                                    pp[:],
                                    ar(X_W2 + buf * 4096 + hh * 1024 + oc * 128, 128),
                                    ar(X_H + hh * 512, 512),
                                    start=(hh == 0), stop=(hh == 3)),
                                   waits=wl, inc="T")
                            to = CNT["T"]
                            if fg == 0:
                                op("vector", lambda e, oc=oc, pp=pp: e.tensor_copy(
                                    ar(X_FACC + oc * 512, 512), pp[:]),
                                   waits=[("T", to)], inc="V")
                            else:
                                op("vector", lambda e, oc=oc, pp=pp: e.tensor_add(
                                    ar(X_FACC + oc * 512, 512),
                                    ar(X_FACC + oc * 512, 512), pp[:]),
                                   waits=[("T", to)], inc="V")
                        w_T_done[buf] = CNT["T"]
                else:
                    for oc in range(8):
                        pp = pos_[oc % 2]
                        for k in range(8):
                            wl = None
                            if k == 0:
                                wl = [("A", CNT["A"])]
                                if oc == 0:
                                    wl += [("DS", cbt_ds), ("V", xnt_v)]
                            op("tensor", lambda e, oc=oc, k=k, pp=pp: e.matmul(
                                pp[:], ar(X_W1 + k * 1024 + oc * 128, 128),
                                ar(X_XNT + k * 512, 512),
                                start=(k == 0), stop=(k == 7)),
                               waits=wl, inc="T")
                        to = CNT["T"]
                        op("scalar", lambda e, oc=oc, pp=pp: e.activation(
                            ar(X_FACC + oc * 512, 512), pp[:], AF.Identity,
                            scale=scb127[:, oc:oc + 1], bias=l1biasc[:, oc:oc + 1]),
                           waits=[("T", to)], inc="A")
                if tt == 0:
                    dump(128, lambda: ar(X_XNT, 4096), 4096)
                    dump(4224, lambda: ar(X_H, 4096), 4096)
                    dump(8320, lambda: ar(X_FACC, 4096), 4096)
                    dbg_done[0] = True
                facc_done = now("V", "A")
                for tix, t in enumerate(tiles):
                    for c in range(8):
                        pe = [ph0, ph1][c % 2]
                        wl = facc_done + [("V", CNT["V"]), ("A", CNT["A"])]
                        op("tensor", lambda e, c=c, tix=tix, pe=pe: e.transpose(
                            pe[:, 0:128], ar(X_FACC + c * 512 + tix * 128, 128), eye[:]),
                           waits=wl, inc="T")
                        tT = CNT["T"]
                        op("vector", lambda e, c=c, pe=pe: e.tensor_copy(
                            stg[:, c * 128:(c + 1) * 128], pe[:, 0:128]),
                           waits=[("T", tT)], inc="V")
                    vstg = CNT["V"]
                    if is_mod:
                        op("scalar", lambda e, t=t: e.activation(
                            xn_buf[:], xs[:, t * D:(t + 1) * D], AF.Identity,
                            scale=r_all[:, t:t + 1], bias=nm_all[:, t:t + 1]),
                           waits=[("V", vstg)], inc="A")
                        op("scalar", lambda e, t=t: e.activation(
                            stg[:], stg[:], AF.Identity, scale=w_mod[:, t:t + 1]),
                           inc="A")
                        aup = CNT["A"]
                        op("vector", lambda e: e.scalar_tensor_tensor(
                            xn_buf[:], xn_buf[:], 1.0, stg[:],
                            op0=ALU.mult, op1=ALU.add),
                           waits=[("A", aup)], inc="V")
                        op("vector", lambda e, t=t: e.scalar_tensor_tensor(
                            xs[:, t * D:(t + 1) * D], xn_buf[:], 0.5,
                            xs[:, t * D:(t + 1) * D], op0=ALU.mult, op1=ALU.add),
                           inc="V")
                    else:
                        op("vector", lambda e, t=t: e.scalar_tensor_tensor(
                            xs[:, t * D:(t + 1) * D], stg[:], 0.5,
                            xs[:, t * D:(t + 1) * D], op0=ALU.mult, op1=ALU.add),
                           waits=[("V", vstg)], inc="V")

    def emit_attn(layer):
        emit_ln()
        A_QT, A_KT, A_VT, A_PV, A_O = 0, 2048, 4096, 6144, 8192
        A_WQKV, A_WOUT, A_RT, A_PT2, A_RNQ, A_OT = 10240, 13312, 14336, 14848, 15872, 15904
        A_XS = 16384
        A_COS, A_SIN, A_BROW = 18432, 20480, 22528
        arena_free = now("T", "V", "A")
        op("sync", lambda e: e.dma_start(
            out=AR[0:1, A_BROW:A_BROW + D],
            in_=wd["l0_bout" if layer == 0 else "l1_boutr"]),
           waits=arena_free, inc="DS")
        if layer == 1:
            op("sync", lambda e: e.dma_start(out=ar(A_COS, S), in_=wd["cosT"]), inc="DS")
            op("sync", lambda e: e.dma_start(out=ar(A_SIN, S), in_=wd["sinT"]), inc="DS")
        with psum([128, 128]) as pt0, psum([128, 128]) as pt1:
            for t in range(NT):
                op("scalar", lambda e, t=t: e.activation(
                    xn_buf[:], xs[:, t * D:(t + 1) * D], AF.Identity,
                    scale=r_all[:, t:t + 1], bias=nm_all[:, t:t + 1]),
                   waits=now("V"), inc="A")
                axn = CNT["A"]
                for c in range(8):
                    pe = [pt0, pt1][c % 2]
                    wl = [("A", axn), ("V", CNT["V"])]
                    if c == 0:
                        wl.append(("DS", CNT["DS"]))
                    op("tensor", lambda e, pe=pe, c=c: e.transpose(
                        pe[:], xn_buf[:, c * 128:(c + 1) * 128], eye[:]),
                       waits=wl, inc="T")
                    tT = CNT["T"]
                    op("vector", lambda e, pe=pe, c=c: e.tensor_copy(
                        stg[:, c * 128:(c + 1) * 128], pe[:]),
                       waits=[("T", tT)], inc="V")
                vstg = CNT["V"]
                op("sync", lambda e, t=t: e.dma_start(
                    out=xnt_d[:, :, t * 128:(t + 1) * 128].rearrange("c p k -> p c k"),
                    in_=stg[:].rearrange("p (c k) -> p c k", k=128)),
                   waits=[("V", vstg)], inc="DS")
        spill_ds = CNT["DS"]

        for pair in range(8):
            wfree = [("T", CNT["T"])] + (arena_free if pair == 0 else [])
            if layer == 0:
                srcs = [wd["l0_wq"][:, pair * 128:(pair + 1) * 128],
                        wd["l0_wkv"][:, pair * 128:(pair + 1) * 128],
                        wd["l0_wkv"][:, D + pair * 128:D + (pair + 1) * 128]]
            else:
                srcs = [wd["l1_qkv"][:, pair * 128:(pair + 1) * 128],
                        wd["l1_qkv"][:, D + pair * 128:D + (pair + 1) * 128],
                        wd["l1_qkv"][:, 2 * D + pair * 128:2 * D + (pair + 1) * 128]]
            for pi, src in enumerate(srcs):
                op("sync", lambda e, pi=pi, src=src: e.dma_start(
                    out=ar(A_WQKV + pi * 1024, 1024).rearrange("p (k f) -> p k f", f=128),
                    in_=src.rearrange("(k p) f -> p k f", p=128)),
                   waits=(wfree if pi == 0 else None), inc="DS")
            op("sync", lambda e, pair=pair: e.dma_start(
                out=ar(A_WOUT, 1024),
                in_=wd["l0_wout" if layer == 0 else "l1_wout"][pair * 128:(pair + 1) * 128, :]),
               inc="DS")
            wds = CNT["DS"]
            with psum([128, 1024]) as pq, psum([128, 1024]) as pk, psum([128, 1024]) as pv:
                for th in range(2):
                    for c in range(8):
                        sbf = c % 2
                        wl = [("DS", spill_ds), ("T", CNT["T"])]
                        op("sync", lambda e, c=c, th=th, sbf=sbf: e.dma_start(
                            out=ar(A_XS + sbf * 1024, 1024),
                            in_=xnt_d[c, :, th * 1024:(th + 1) * 1024]),
                           waits=wl, inc="DS")
                        xds = CNT["DS"]
                        for pi, pp in enumerate([pq, pk, pv]):
                            for mv in range(2):
                                wl2 = None
                                if pi == 0 and mv == 0:
                                    wl2 = [("DS", xds)]
                                    if c == 0:
                                        wl2.append(("A", CNT["A"]))
                                op("tensor", lambda e, pi=pi, mv=mv, c=c, sbf=sbf, pp=pp: e.matmul(
                                    pp[:, mv * 512:(mv + 1) * 512],
                                    ar(A_WQKV + pi * 1024 + c * 128, 128),
                                    ar(A_XS + sbf * 1024 + mv * 512, 512),
                                    start=(c == 0), stop=(c == 7),
                                    skip_group_check=True),
                                   waits=wl2, inc="T")
                    tproj = CNT["T"]
                    for pi, (dst, pp) in enumerate([(A_QT, pq), (A_KT, pk), (A_PV, pv)]):
                        if layer == 1:
                            bc = qkvbc[:, pi * 8 + pair:pi * 8 + pair + 1]
                            op("scalar", lambda e, dst=dst, th=th, pp=pp, bc=bc: e.activation(
                                ar(dst + th * 1024, 1024), pp[:], AF.Identity,
                                scale=1.0, bias=bc),
                               waits=[("T", tproj)], inc="A")
                        else:
                            op("scalar", lambda e, dst=dst, th=th, pp=pp: e.copy(
                                ar(dst + th * 1024, 1024), pp[:]),
                               waits=[("T", tproj)], inc="A")
            aproj = CNT["A"]
            with psum([128, 512]) as ptv:
                for g in range(4):
                    wl = [("A", aproj), ("V", CNT["V"])]
                    for i in range(4):
                        kt = g * 4 + i
                        op("tensor", lambda e, kt=kt, i=i, ptv=ptv: e.transpose(
                            ptv[:, i * 128:(i + 1) * 128],
                            ar(A_PV + kt * 128, 128), eye[:]),
                           waits=(wl if i == 0 else None), inc="T")
                    tg = CNT["T"]
                    op("vector", lambda e, g=g, ptv=ptv: e.tensor_copy(
                        ar(A_VT + g * 512, 512), ptv[:]),
                       waits=[("T", tg)], inc="V")

            if layer == 0:
                rn_q = AR[0:2, A_O:A_O + 2048]
                rn_k = AR[0:2, A_XS:A_XS + 2048]
                with psum([2, 512]) as pn0, psum([2, 512]) as pn1, \
                     psum([128, 512]) as pnr, psum([128, 2]) as pnt:
                    pns = [pn0, pn1]
                    for src, rn in [(A_QT, rn_q), (A_KT, rn_k)]:
                        for i in range(4):
                            op("scalar", lambda e, src=src, i=i: e.activation(
                                stg[:, 0:512], ar(src + i * 512, 512), AF.Square),
                               waits=now("V", "T"), inc="A")
                            asq = CNT["A"]
                            pn = pns[i % 2]
                            op("tensor", lambda e, pn=pn: e.matmul(
                                pn[:], bd2[:], stg[:, 0:512], start=True, stop=True),
                               waits=[("A", asq)], inc="T")
                            tn = CNT["T"]
                            op("scalar", lambda e, rn=rn, i=i, pn=pn: e.activation(
                                rn[:, i * 512:(i + 1) * 512], pn[:], AF.Sqrt),
                               waits=[("T", tn)], inc="A")
                    az = CNT["A"]
                    op("vector", lambda e: e.tensor_scalar(
                        rn_q[:], rn_q[:], 1e-12, None, op0=ALU.max),
                       waits=[("A", az)], inc="V")
                    op("vector", lambda e: e.tensor_scalar(
                        rn_k[:], rn_k[:], 1e-12, None, op0=ALU.max), inc="V")
                    op("vector", lambda e: e.reciprocal(rn_q[:], rn_q[:]), inc="V")
                    op("vector", lambda e: e.reciprocal(rn_k[:], rn_k[:]), inc="V")
                    op("vector", lambda e, pair=pair: e.tensor_scalar(
                        rn_q[:], rn_q[:], eSp[:, pair:pair + 1], None, op0=ALU.mult), inc="V")
                    vrn = CNT["V"]
                    for i in range(4):
                        op("tensor", lambda e, i=i, pnr=pnr: e.matmul(
                            pnr[:], arep[:], rn_k[:, i * 512:(i + 1) * 512],
                            start=True, stop=True),
                           waits=[("V", vrn if i == 0 else CNT["V"])], inc="T")
                        tr = CNT["T"]
                        op("vector", lambda e, i=i, pnr=pnr: e.tensor_tensor(
                            ar(A_KT + i * 512, 512), ar(A_KT + i * 512, 512),
                            pnr[:], op=ALU.mult),
                           waits=[("T", tr)], inc="V")
                    for qt in range(16):
                        op("tensor", lambda e, qt=qt, pnt=pnt: e.transpose(
                            pnt[:], rn_q[:, qt * 128:(qt + 1) * 128],
                            eye[0:2, 0:2]),
                           waits=[("V", CNT["V"])], inc="T")
                        tq = CNT["T"]
                        op("vector", lambda e, qt=qt, pnt=pnt: e.tensor_copy(
                            ar(A_RNQ + qt * 2, 2), pnt[:]),
                           waits=[("T", tq)], inc="V")
            else:
                with psum([128, 512]) as pr:
                    for src in (A_QT, A_KT):
                        for i in range(4):
                            op("tensor", lambda e, src=src, i=i, pr=pr: e.matmul(
                                pr[:], pxor[:], ar(src + i * 512, 512),
                                start=True, stop=True),
                               waits=[("A", aproj), ("V", CNT["V"])], inc="T")
                            tp = CNT["T"]
                            op("vector", lambda e, i=i: e.tensor_tensor(
                                ar(A_RT, 512), pr[:], ar(A_SIN + i * 512, 512),
                                op=ALU.mult),
                               waits=[("T", tp)], inc="V")
                            op("vector", lambda e, src=src, i=i: e.tensor_tensor(
                                ar(src + i * 512, 512), ar(src + i * 512, 512),
                                ar(A_COS + i * 512, 512), op=ALU.mult), inc="V")
                            op("vector", lambda e, src=src, i=i: e.tensor_add(
                                ar(src + i * 512, 512), ar(src + i * 512, 512),
                                ar(A_RT, 512)), inc="V")
            prep_v = CNT["V"]
            prep_a = CNT["A"]

            with psum([128, 2048]) as psim, psum([128, 512]) as ppt, \
                 psum([128, 64]) as po, psum([128, 1024]) as py:
                for h in range(2):
                    hb = h * 64
                    for qt in range(16):
                        wl = ([("V", prep_v), ("A", prep_a)]
                              if (h == 0 and qt == 0) else [("A", CNT["A"])])
                        for ks in range(4):
                            op("tensor", lambda e, hb=hb, qt=qt, ks=ks, psim=psim: e.matmul(
                                psim[:, ks * 512:(ks + 1) * 512],
                                AR[hb:hb + 64, A_QT + qt * 128:A_QT + qt * 128 + 128],
                                AR[hb:hb + 64, A_KT + ks * 512:A_KT + (ks + 1) * 512],
                                start=True, stop=True, skip_group_check=True),
                               waits=(wl if ks == 0 else None), inc="T")
                        tsim = CNT["T"]
                        op("vector", lambda e, psim=psim: e.reduce_max(
                            cm1[:], psim[:], axis=AX),
                           waits=[("T", tsim)], inc="V")
                        if layer == 0:
                            cq = ar(A_RNQ + qt * 2 + h, 1)
                            op("vector", lambda e, cq=cq: e.scalar_tensor_tensor(
                                sumc[:, 0:1], cm1[:], -1.0, cq,
                                op0=ALU.mult, op1=ALU.mult),
                               waits=[("A", CNT["A"])] + now("V"), inc="V")
                            vexp = CNT["V"]
                            op("scalar", lambda e, cq=cq, psim=psim: e.activation(
                                ar(A_PV, 2048), psim[:], AF.Exp, scale=cq,
                                bias=sumc[:, 0:1]),
                               waits=[("V", vexp)], inc="A")
                        else:
                            op("vector", lambda e: e.tensor_scalar(
                                sumc[:, 0:1], cm1[:], -0.125, None, op0=ALU.mult),
                               waits=[("A", CNT["A"])] + now("V"), inc="V")
                            vexp = CNT["V"]
                            op("scalar", lambda e, psim=psim: e.activation(
                                ar(A_PV, 2048), psim[:], AF.Exp, scale=0.125,
                                bias=sumc[:, 0:1]),
                               waits=[("V", vexp)], inc="A")
                        aexp = CNT["A"]
                        op("vector", lambda e: e.reduce_sum(sumc[:, 1:2], ar(A_PV, 2048), axis=AX),
                           waits=[("A", aexp)], inc="V")
                        op("vector", lambda e: e.reciprocal(sumc[:, 0:1], sumc[:, 1:2]),
                           waits=now("V"), inc="V")
                        for g in range(4):
                            pb = (g % 2) * 512
                            wl2 = [("A", aexp)]
                            if g >= 1:
                                wl2.append(("V", CNT["V"]))
                            for i in range(4):
                                kt = g * 4 + i
                                op("tensor", lambda e, kt=kt, i=i, ppt=ppt: e.transpose(
                                    ppt[:, i * 128:(i + 1) * 128],
                                    ar(A_PV + kt * 128, 128), eye[:]),
                                   waits=(wl2 if i == 0 else None), inc="T")
                            tg = CNT["T"]
                            op("vector", lambda e, pb=pb, ppt=ppt: e.tensor_copy(
                                ar(A_PT2 + pb, 512), ppt[:]),
                               waits=[("T", tg)], inc="V")
                            vg = CNT["V"]
                            for i in range(4):
                                kt = g * 4 + i
                                op("tensor", lambda e, kt=kt, i=i, pb=pb, hb=hb, po=po: e.matmul(
                                    po[:],
                                    ar(A_PT2 + pb + i * 128, 128),
                                    ar(A_VT + kt * 128 + hb, 64),
                                    start=(kt == 0), stop=(kt == 15),
                                    skip_group_check=True),
                                   waits=([("V", vg)] if i == 0 else None), inc="T")
                        tav = CNT["T"]
                        vr = CNT["V"]
                        op("scalar", lambda e, qt=qt, hb=hb, po=po: e.activation(
                            ar(A_O + qt * 128 + hb, 64), po[:], AF.Identity,
                            scale=sumc[:, 0:1]),
                           waits=[("T", tav), ("V", vr)], inc="A")
                oa = CNT["A"]
                for qt in range(16):
                    wl = [("A", oa)]
                    if qt > 0:
                        wl += [("T", CNT["T"]), ("V", CNT["V"])]
                    op("tensor", lambda e, qt=qt, ppt=ppt: e.transpose(
                        ppt[:, 0:128], ar(A_O + qt * 128, 128), eye[:]),
                       waits=wl, inc="T")
                    tot = CNT["T"]
                    op("vector", lambda e, ppt=ppt: e.tensor_copy(
                        ar(A_OT, 128), ppt[:, 0:128]),
                       waits=[("T", tot)], inc="V")
                    vot = CNT["V"]
                    for mv in range(2):
                        op("tensor", lambda e, mv=mv, py=py: e.matmul(
                            py[:, mv * 512:(mv + 1) * 512],
                            ar(A_OT, 128),
                            ar(A_WOUT + mv * 512, 512),
                            start=True, stop=(pair != 0),
                            skip_group_check=True),
                           waits=([("V", vot), ("DS", wds)] if mv == 0 else None), inc="T")
                    if pair == 0:
                        for mv in range(2):
                            op("tensor", lambda e, mv=mv, py=py: e.matmul(
                                py[:, mv * 512:(mv + 1) * 512],
                                cones[0:1, :],
                                AR[0:1, A_BROW + mv * 512:A_BROW + (mv + 1) * 512],
                                start=False, stop=True,
                                skip_group_check=True), inc="T")
                    ty = CNT["T"]
                    op("vector", lambda e, qt=qt, py=py: e.tensor_add(
                        xs[:, qt * D:(qt + 1) * D], xs[:, qt * D:(qt + 1) * D],
                        py[:]),
                       waits=[("T", ty)], inc="V")

    store_total = 0
    for b in range(NB_RUN):
        xwait = [("DG", CNT["DG"])] if b > 0 else []
        for t in range(NT):
            op("sync", lambda e, b=b, t=t: e.dma_start(
                out=xs[:, t * D:(t + 1) * D],
                in_=xin[b, t * 128:(t + 1) * 128, :]),
               waits=(xwait if t == 0 else None), inc="DS")
        ld_ds = CNT["DS"]
        op("vector", lambda e: e.nop(), waits=[("DS", ld_ds)], inc="V")
        op("scalar", lambda e: e.nop(), waits=[("DS", ld_ds)], inc="A")

        for (layer, kind) in SUBLAYERS[:PARTS]:
            if kind == "ffn":
                emit_ffn(layer, is_mod=(layer == 0))
            else:
                emit_attn(layer)

        vdone = now("V")
        for t in range(NT):
            op("gpsimd", lambda e, b=b, t=t: e.dma_start(
                out=yout[b, t * 128:(t + 1) * 128, :],
                in_=xs[:, t * D:(t + 1) * D]),
               waits=(vdone if t == 0 else None), inc="DG")
        store_total = CNT["DG"]

    final_dg = store_total

    with es:
        with nc.semaphore("semT") as sT, nc.semaphore("semV") as sV, \
             nc.semaphore("semA") as sA, nc.semaphore("semG") as sG, \
             nc.semaphore("semDS") as sDS, nc.semaphore("semDG") as sDG:
            _nums = sorted(x.num for x in (sT, sV, sA, sG, sDS, sDG))
            assert _nums == list(range(_nums[0], _nums[-1] + 1)), _nums
            nc.gpsimd.sem_clear(range(_nums[0], _nums[-1] + 1))
            nc.all_engine_barrier()
            block = es.enter_context(nc.Block())
            sems = {"T": sT, "V": sV, "A": sA, "G": sG, "DS": sDS, "DG": sDG}

            @block.tensor
            def _(e):
                for f in ops["tensor"]:
                    f(e, sems)

            @block.vector
            def _(e):
                for f in ops["vector"]:
                    f(e, sems)

            @block.scalar
            def _(e):
                for f in ops["scalar"]:
                    f(e, sems)

            @block.sync
            def _(e):
                for f in ops["sync"]:
                    f(e, sems)

            @block.gpsimd
            def _(e):
                for f in ops["gpsimd"]:
                    f(e, sems)
                e.wait_ge(sems["DG"], final_dg)
    return nc


def kernel(**inputs):
    n = 8
    c = _host_consts()
    x = np.ascontiguousarray(np.asarray(inputs["L0_attn_x"]))
    common = {
        "l0_wq": np.ascontiguousarray(inputs["L0_attn_w_q"]),
        "l0_wkv": np.ascontiguousarray(inputs["L0_attn_w_kv"]),
        "l0_wout": np.ascontiguousarray(inputs["L0_attn_w_out"]),
        "l0_bout": np.ascontiguousarray(np.asarray(inputs["L0_attn_b_out"]).reshape(1, D)),
        "l0_scale": np.ascontiguousarray(np.asarray(inputs["L0_attn_scale"]).reshape(8, 2).T),
        "wr_rep": np.ascontiguousarray(
            np.broadcast_to(np.asarray(inputs["L0_ffn_w_router"])[None, :], (128, D))),
        "l0_w1": np.ascontiguousarray(inputs["L0_ffn_W1"]),
        "l0_b1c": np.ascontiguousarray(np.asarray(inputs["L0_ffn_b1"]).reshape(32, 128).T),
        "l0_w2": np.ascontiguousarray(inputs["L0_ffn_W2"]),
        "l0_b2c": np.ascontiguousarray(np.asarray(inputs["L0_ffn_b2"]).reshape(8, 128).T),
        "l1_qkv": np.ascontiguousarray(inputs["L1_attn_qkv_w"]),
        "l1_qkvbc": np.ascontiguousarray(np.asarray(inputs["L1_attn_qkv_b"]).reshape(24, 128).T),
        "l1_wout": np.ascontiguousarray(inputs["L1_attn_out_w"]),
        "l1_boutr": np.ascontiguousarray(np.asarray(inputs["L1_attn_out_b"]).reshape(1, D)),
        "l1_cbt": np.ascontiguousarray(np.asarray(inputs["L1_ffn_CB"]).T),
        "l1_scbc": np.ascontiguousarray(np.asarray(inputs["L1_ffn_SCB"]).reshape(8, 128).T),
        "l1_biasc": np.ascontiguousarray(np.asarray(inputs["L1_ffn_bias"]).reshape(8, 128).T),
    }
    common.update({k: np.ascontiguousarray(v) for k, v in c.items()})
    nc = build_nc()
    in_maps = []
    for i in range(n):
        m = dict(common)
        m["xin"] = np.ascontiguousarray(x[2 * i:2 * i + 2])
        in_maps.append(m)
    res = run_bass_kernel_spmd(nc, in_maps, core_ids=list(range(n)))
    return np.concatenate([r["yout"] for r in res.results], axis=0)
